# version 06
# speedup vs baseline: 1.2586x; 1.0003x over previous
import numpy as np
from contextlib import ExitStack

import concourse.bass as bass
import concourse.mybir as mybir
from concourse.bass_utils import run_bass_kernel_spmd

F32 = mybir.dt.float32
BF16 = mybir.dt.bfloat16
I8 = mybir.dt.int8
I32 = mybir.dt.int32
AF = mybir.ActivationFunctionType
ALU = mybir.AluOpType
AX = mybir.AxisListType

D_MODEL = 1024
D_INNER = 2048
D_STATE = 16
D_CONV = 4
FFN_INNER = 4096
B_FULL = 4
T_FULL = 8192
T_OWN = 4096
PFX = 128
T_ALL = T_OWN + PFX
N_CORES = 8
TC_W = 1024
N_TC = 4
SC = 512
N_SCC = 16
NG = 4
NGRP = 4
SEG = SC + 1
DH = 1024
N_DT = 8
EPS = 1e-5

STAGE = "full"
DBG = []
SCAN_SERIAL = False


class Q:
    def __init__(self, ctx, nc, name):
        self.h = ctx.enter_context(nc.semaphore(name))
        self.n = 0
        self.name = name


class Ticket:
    __slots__ = ("q", "v")

    def __init__(self, q, v):
        self.q, self.v = q, v


class Em:
    def __init__(self, nc, ctx):
        self.nc = nc
        self.ctx = ctx
        self.ops = {e: [] for e in ("sync", "tensor", "vector", "scalar",
                                    "gpsimd")}
        self.seen = {e: {} for e in self.ops}
        self.counts = {e: 0 for e in self.ops}
        self.nops = {e: 0 for e in self.ops}
        self.consumed = {}
        self.engq = {}

    def q(self, name):
        return Q(self.ctx, self.nc, name)

    def emit(self, eng, fn, waits=(), inc=None, inck=1):
        todo = []
        flat = []

        def _fl(x):
            if x is None:
                return
            if isinstance(x, (list, tuple)):
                for y in x:
                    _fl(y)
            else:
                flat.append(x)
        _fl(list(waits))
        byq = {}
        for t in flat:
            if t.v > byq.get(t.q, -1):
                byq[t.q] = t.v
        for q_, v_ in byq.items():
            if v_ > self.consumed.get(q_, 0):
                self.consumed[q_] = v_
            if v_ > self.seen[eng].get(q_, -1):
                todo.append((q_, v_))
                self.seen[eng][q_] = v_
        tick = None
        if inc is not None:
            inc.n += inck
            tick = Ticket(inc, inc.n)
        self.counts[eng] += 1 + len(todo)
        self.nops[eng] += 1

        def op(e, todo=tuple(todo), fn=fn, inc=inc, inck=inck):
            for q, v in todo:
                e.wait_ge(q.h, v)
            ins = fn(e)
            if inc is not None:
                ins.then_inc(inc.h, inck)
        self.ops[eng].append(op)
        return tick

    def flush(self):
        with self.nc.Block() as block:
            @block.sync
            def _(eng):
                for fn in self.ops["sync"]:
                    fn(eng)

            @block.tensor
            def _(eng):
                for fn in self.ops["tensor"]:
                    fn(eng)

            @block.vector
            def _(eng):
                for fn in self.ops["vector"]:
                    fn(eng)

            @block.scalar
            def _(eng):
                for fn in self.ops["scalar"]:
                    fn(eng)

            @block.gpsimd
            def _(eng):
                for fn in self.ops["gpsimd"]:
                    fn(eng)


def _slice_inputs(inputs):
    x = np.asarray(inputs["L0_attn_x"])
    conv_w = np.asarray(inputs["L0_attn_conv_w"]).reshape(D_INNER, D_CONV)
    maps = []
    for core in range(N_CORES):
        pair, rank = core // 2, core % 2
        t0 = rank * T_OWN
        xs = np.zeros((T_ALL, D_MODEL), dtype=np.float32)
        if rank == 0:
            xs[PFX:] = x[pair, 0:T_OWN]
        else:
            xs[:] = x[pair, t0 - PFX:t0 + T_OWN]
        maps.append({
            "x_in": np.ascontiguousarray(xs),
            "w_inproj": np.asarray(inputs["L0_attn_in_proj_w"]),
            "w_conv": np.ascontiguousarray(conv_w),
            "b_conv": np.ascontiguousarray(
                np.asarray(inputs["L0_attn_conv_b"]).reshape(-1, 1)),
            "w_dt": np.asarray(inputs["L0_attn_dt_w"]),
            "b_dt": np.ascontiguousarray(
                np.asarray(inputs["L0_attn_dt_b"]).reshape(-1, 1)),
            "w_bproj": np.asarray(inputs["L0_attn_B_proj_w"]),
            "w_cproj": np.asarray(inputs["L0_attn_C_proj_w"]),
            "a_log": np.ascontiguousarray(
                np.asarray(inputs["L0_attn_A_log"])[rank * DH:(rank + 1) * DH]),
            "d_prm": np.ascontiguousarray(
                np.asarray(inputs["L0_attn_D_param"]).reshape(-1, 1)),
            "w_outproj": np.asarray(inputs["L0_attn_out_proj_w"]),
            "w_gate": np.asarray(inputs["L0_ffn_w_gate"]),
            "w_up": np.asarray(inputs["L0_ffn_w_up"]),
            "w_down": np.asarray(inputs["L0_ffn_w_down"]),
            "meta": np.array([[rank * DH, rank * 4, rank * 2048,
                               rank * 8192]], dtype=np.int32),
        })
    return maps


def kernel(**inputs):
    nc = build_program()
    in_maps = _slice_inputs(inputs)
    res = run_bass_kernel_spmd(nc, in_maps, core_ids=list(range(N_CORES)))
    full = np.empty((B_FULL, T_FULL, D_MODEL), dtype=np.float32)
    for core in range(N_CORES):
        pair, rank = core // 2, core % 2
        full[pair, rank * T_OWN:(rank + 1) * T_OWN] = res.results[core]["out"]
    if DBG:
        kernel.dbg = [{k: v for k, v in r.items() if k.startswith("dbg_")}
                      for r in res.results]
    return full


def build_program():
    nc = bass.Bass("TRN2", target_bir_lowering=False, debug=False)
    with ExitStack() as ctx:
        em = Em(nc, ctx)
        b = Builder(nc, ctx, em)
        b.declare()
        b.prep()
        b.main()
        b.finish()
        em.flush()
    tot = sum(em.counts.values())
    print(f"[kernel-build] instrs: {em.counts} total={tot} ops={em.nops}")
    return nc


class Builder:
    def __init__(self, nc, ctx, em):
        self.nc, self.ctx, self.em = nc, ctx, em
        self.free = {}

    def declare(self):
        nc, ctx = self.nc, self.ctx

        def din(name, shape, dtype=F32):
            return nc.dram_tensor(name, shape, dtype, kind="ExternalInput").ap()

        def dint(name, shape, dtype=F32):
            return nc.dram_tensor(name, shape, dtype, kind="Internal").ap()

        s = self
        s.x_in = din("x_in", [T_ALL, D_MODEL])
        s.w_inproj = din("w_inproj", [2 * D_INNER, D_MODEL])
        s.w_conv = din("w_conv", [D_INNER, D_CONV])
        s.b_conv = din("b_conv", [D_INNER, 1])
        s.w_dt = din("w_dt", [D_INNER, D_INNER])
        s.b_dt = din("b_dt", [D_INNER, 1])
        s.w_bproj = din("w_bproj", [D_STATE, D_INNER])
        s.w_cproj = din("w_cproj", [D_STATE, D_INNER])
        s.a_log = din("a_log", [DH, D_STATE])
        s.d_prm = din("d_prm", [D_INNER, 1])
        s.w_outproj = din("w_outproj", [D_MODEL, D_INNER])
        s.w_gate = din("w_gate", [FFN_INNER, D_MODEL])
        s.w_up = din("w_up", [FFN_INNER, D_MODEL])
        s.w_down = din("w_down", [D_MODEL, FFN_INNER])
        s.meta = din("meta", [1, 4], I32)
        s.out = nc.dram_tensor("out", [T_OWN, D_MODEL], F32,
                               kind="ExternalOutput").ap()

        s.gate_q = dint("gate_q", [FFN_INNER, D_MODEL], BF16)
        s.up_q = dint("up_q", [FFN_INNER, D_MODEL], BF16)
        s.down_q = dint("down_q", [D_MODEL, FFN_INNER], BF16)
        s.inproj_b = dint("inproj_b", [2 * D_INNER, D_MODEL], BF16)
        s.dt_w16 = dint("dt_w16", [D_INNER, D_INNER], BF16)
        s.outproj_b = dint("outproj_b", [D_MODEL, D_INNER], BF16)
        s.bc_b16 = dint("bc_b16", [2 * D_STATE, D_INNER], BF16)
        s.lnq_d = dint("lnq_d", [T_ALL, D_MODEL], BF16)
        s.ln2_d = dint("ln2_d", [T_ALL, D_MODEL], BF16)
        s.ln3_d = dint("ln3_d", [T_OWN, D_MODEL], BF16)
        s.graw_d = dint("graw_d", [FFN_INNER, TC_W + PFX], F32)
        s.ginq1_d = dint("ginq1_d", [FFN_INNER, T_ALL], BF16)
        s.ginq2_d = dint("ginq2_d", [FFN_INNER, T_OWN], BF16)
        s.x1_d = dint("x1_d", [T_ALL, D_MODEL], F32)
        s.x2_d = dint("x2_d", [T_OWN, D_MODEL], F32)
        s.z_d = dint("z_d", [D_INNER, T_OWN], F32)
        s.xp_d = dint("xp_d", [D_INNER, T_ALL], F32)
        s.ypost_d = dint("ypost_d", [D_INNER, T_OWN], BF16)
        s.s1row1_d = dint("s1row1_d", [1, T_ALL], F32)
        s.s1row2_d = dint("s1row2_d", [1, T_OWN], F32)
        s.s2row1_d = dint("s2row1_d", [1, T_ALL], F32)
        s.s2row2_d = dint("s2row2_d", [1, T_OWN], F32)
        s.swrow_d = dint("swrow_d", [8, 1], F32)
        AGR = 2 * D_INNER + 2 * D_STATE
        s.AGR = AGR
        s.ag_in = [dint(f"ag_in{k}", [AGR, TC_W]) for k in range(N_TC)]
        s.ag_out = [dint(f"ag_out{k}", [2 * AGR, TC_W]) for k in range(N_TC)]
        s.bc16_d = [dint(f"bc16_{k}", [64, TC_W], BF16) for k in range(N_TC)]
        s.yag_in = [dint(f"yag_in{c}", [DH, TC_W]) for c in range(8)]
        s.yag_all = dint("yag_all", [8, 2 * DH, TC_W])
        s.sc_d = [dint(f"sc_d{k}", [2 * 2048, TC_W]) for k in range(N_TC)]
        s.yc_d = dint("yc_d", [8192, TC_W])
        s.gate_qT = dint("gate_qT", [D_MODEL, FFN_INNER], BF16)
        s.up_qT = dint("up_qT", [D_MODEL, FFN_INNER], BF16)
        s.down_qT = dint("down_qT", [FFN_INNER, D_MODEL], BF16)
        s.inprojT = dint("inprojT", [D_MODEL, 2 * D_INNER], BF16)
        s.dt_wT = dint("dt_wT", [D_INNER, D_INNER], BF16)
        s.outprojT = dint("outprojT", [D_INNER, D_MODEL], BF16)
        s.bcT = dint("bcT", [D_INNER, 2 * D_STATE], BF16)

        s.dbg_out = {}
        for nm in DBG:
            shp, dt_ = {
                "x1": ([T_ALL, D_MODEL], F32),
                "lnq": ([T_ALL, D_MODEL], BF16),
                "ag": ([2 * DH + D_INNER + 2 * D_STATE, T_OWN], F32),
                "graw": ([FFN_INNER, TC_W + PFX], F32),
                "ginq": ([FFN_INNER, T_ALL], BF16),
                "s2": ([2, T_ALL], F32),
                "y": ([DH, T_FULL], F32),
                "sc": ([4096, TC_W], F32),
                "x2": ([T_OWN, D_MODEL], F32),
            }[nm]
            s.dbg_out[nm] = nc.dram_tensor("dbg_" + nm, shp, dt_,
                                           kind="ExternalOutput").ap()

        def sbuf(name, shape, dtype):
            t = ctx.enter_context(nc.sbuf_tensor(name, shape, dtype))
            setattr(self, name, t)

        sbuf("identf", [128, 128], F32)
        sbuf("onescol", [128, 1], F32)
        sbuf("onesb", [1, 128], BF16)
        sbuf("monesb", [1, 128], BF16)
        sbuf("prepbuf", [128, 288], F32)
        sbuf("apos", [128, N_DT, D_STATE], F32)
        sbuf("convw", [128, 16, D_CONV], F32)
        sbuf("convb", [128, 16], F32)
        sbuf("dtbias", [128, 16], F32)
        sbuf("dprm_t", [128, 16], F32)
        sbuf("swinv", [128, 8], F32)
        sbuf("meta_sb", [1, 4], I32)
        sbuf("constv", [128, 2], F32)
        sbuf("rowstage", [16, 128], F32)
        NTI = 10
        self.NTI = NTI
        for nm in ("st_sum", "st_sq", "st_m", "st_var", "st_rstd", "st_mx",
                   "st_s1", "st_rs1", "st_bs1", "st_b0", "st_t0", "st_t1",
                   "st_s1inv", "st_mx2", "st_f", "st_s2a", "st_dq"):
            sbuf(nm, [128, 2, NTI], F32)
        sbuf("xin", [128, 2, D_MODEL], F32)
        sbuf("aq", [128, 1, D_MODEL], F32)
        sbuf("c8a", [128, 1, D_MODEL], I8)
        sbuf("dda", [128, 1, D_MODEL], F32)
        sbuf("r1a", [128, 1, D_MODEL], F32)
        sbuf("xqn", [128, 1, D_MODEL], BF16)
        sbuf("lnT", [128, 8, 512], BF16)
        sbuf("wst", [128, 2, 8, 512], BF16)
        sbuf("gst", [128, 16, 256], BF16)
        sbuf("gsil", [128, 1024], F32)
        sbuf("graw", [128, 1024], F32)
        sbuf("gabs", [128, 1024], F32)
        sbuf("macc", [128, 2, 1152], F32)
        sbuf("gqn", [128, 1024], BF16)
        sbuf("srowb", [128, 512], F32)
        sbuf("xres", [128, 2, 512], F32)
        sbuf("x1t", [128, 2, 512], F32)
        sbuf("cxp", [128, 1, 515], F32)
        sbuf("cg0", [128, 512], F32)
        sbuf("cp1", [128, 512], F32)
        sbuf("cm", [128, 512], F32)
        sbuf("xsst", [128, 512], F32)
        sbuf("xsb", [128, 16, 512], BF16)
        sbuf("wbc", [128, 16, 32], BF16)
        sbuf("sgt", [128, 1, 512], F32)
        sbuf("bcst", [32, 512], F32)
        sbuf("ndtb", [128, 16], F32)
        sbuf("identb", [128, 128], BF16)
        sbuf("dtna", [128, 2, 512], F32)
        sbuf("ua", [128, 2, 512], BF16)
        sbuf("scA", [128, 2, NG, 513], F32)
        sbuf("scB", [128, NG, 513], F32)
        sbuf("scH", [128, NG, 513], F32)
        sbuf("hcall", [128, 2, NG, 512], BF16)
        sbuf("bfull", [128, D_STATE, 512], BF16)
        sbuf("cfull", [128, D_STATE, 512], BF16)
        sbuf("carry", [128, 8, 16], F32)
        sbuf("yst", [128, 512], F32)
        sbuf("ypb", [128, 512], BF16)

        s.pTT = ctx.enter_context(nc.psum_tensor("pTT", [128, 1024], F32))
        s.pUU = ctx.enter_context(nc.psum_tensor("pUU", [128, 1024], F32))
        s.pYY = ctx.enter_context(nc.psum_tensor("pYY", [128, 1024], F32))
        s.pS0 = ctx.enter_context(nc.psum_tensor("pS0", [128, 512], F32))
        self.banks = {"pT0": s.pTT[:, 0:512], "pT1": s.pTT[:, 512:1024],
                      "pT2": s.pUU[:, 0:512], "pT3": s.pUU[:, 512:1024],
                      "pY0": s.pYY[:, 0:512], "pY1": s.pYY[:, 512:1024],
                      "pS0": s.pS0[:, 0:512]}

        em = self.em
        RQ = 6
        self.qsems = {
            "sync": [em.q(f"dsp{i}") for i in range(RQ)],
            "scalar": [em.q(f"dact{i}") for i in range(RQ)],
            "gpsimd": [em.q(f"dgp{i}") for i in range(RQ)],
        }
        self.qcount = {"sync": 0, "scalar": 0, "gpsimd": 0}
        self.RQ = RQ
        self.gq = {}
        self.gacked = {}
        self.ccq = [em.q("ccq0"), em.q("ccq1")]
        self.ccn = 0
        self.pe = em.q("pe")
        self.rgq = em.q("rgq")
        self.ve = em.q("ve")
        self.sc = em.q("sc")
        self.gp = em.q("gp")
        em.engq = {"tensor": self.pe, "vector": self.ve,
                   "scalar": self.sc, "gpsimd": self.gp}

    def dma(self, ring, out, in_, waits=(), transpose=False, slow=False,
            key=None):
        if key is not None:
            q = self.gq.get(key)
            if q is None:
                q = self.em.q(f"dk{len(self.gq)}")
                self.gq[key] = q
                self.gacked[key] = 0
            extra = []
            cons = self.em.consumed.get(q, 0)
            if cons > self.gacked[key]:
                extra = [Ticket(q, cons)]
                self.gacked[key] = cons
        else:
            i = self.qcount[ring]
            self.qcount[ring] = i + 1
            q = self.qsems[ring][i % self.RQ]
            extra = []
            if i >= self.RQ:
                extra = [Ticket(q, q.n)]
        if transpose:
            fn = (lambda e, o=out, i_=in_:
                  e.dma_start(out=o, in_=i_, transpose=True))
        elif slow:
            fn = (lambda e, o=out, i_=in_:
                  e.dma_start(out=o, in_=i_, allow_slow_non_contiguous=True))
        else:
            fn = lambda e, o=out, i_=in_: e.dma_start(out=o, in_=i_)
        return self.em.emit(ring, fn, waits=list(waits) + extra,
                            inc=q, inck=16)

    def v(self, fn, waits=()):
        return self.em.emit("vector", fn, waits=waits, inc=self.ve)

    def s(self, fn, waits=()):
        return self.em.emit("scalar", fn, waits=waits, inc=self.sc)

    def g(self, fn, waits=()):
        return self.em.emit("gpsimd", fn, waits=waits, inc=self.gp)

    def t(self, fn, waits=()):
        return self.em.emit("tensor", fn, waits=waits, inc=self.pe)

    PCOVER = {"pTT": ("pTT", "pT0", "pT1"), "pUU": ("pUU", "pT2", "pT3"),
              "pYY": ("pYY", "pY0", "pY1"), "pS0": ("pS0",),
              "pT0": ("pTT", "pT0"), "pT1": ("pTT", "pT1"),
              "pT2": ("pUU", "pT2"), "pT3": ("pUU", "pT3"),
              "pY0": ("pYY", "pY0"), "pY1": ("pYY", "pY1")}
    PSET = {"pTT": ("pTT", "pT0", "pT1"), "pUU": ("pUU", "pT2", "pT3"),
            "pYY": ("pYY", "pY0", "pY1"), "pS0": ("pS0",),
            "pT0": ("pT0",), "pT1": ("pT1",), "pT2": ("pT2",),
            "pT3": ("pT3",), "pY0": ("pY0",), "pY1": ("pY1",)}

    def bankw(self, nm):
        return [self.free.get(("bank", x)) for x in self.PCOVER[nm]]

    def bankset(self, nm, tick):
        for x in self.PSET[nm]:
            self.free[("bank", x)] = tick

    def prep(self):
        s = self
        s.t_meta = s.dma("sync", s.meta_sb[:, :], s.meta[:, :])

        t_ic = s.g(lambda e: e.iota(s.prepbuf[:, 0:128], [[1, 128]],
                                    base=0, channel_multiplier=0,
                                    allow_small_or_imprecise_dtypes=True))
        t_ip = s.g(lambda e: e.iota(s.prepbuf[:, 128:129], [[1, 1]],
                                    base=0, channel_multiplier=1,
                                    allow_small_or_imprecise_dtypes=True))
        s.t_ident = s.v(lambda e: e.tensor_scalar(
            s.identf[:, :], s.prepbuf[:, 0:128], s.prepbuf[:, 128:129],
            None, ALU.is_equal), waits=[t_ic, t_ip])
        t_o1 = s.g(lambda e: e.memset(s.onescol[:, :], 1.0))
        t_o2 = s.g(lambda e: e.memset(s.onesb[:, :], 1.0))
        t_o3 = s.g(lambda e: e.memset(s.monesb[:, :], -1.0))
        t_o4 = s.g(lambda e: e.memset(s.constv[:, 0:1], EPS))
        s.t_ones = [t_o1, t_o2, t_o3]
        s.t_eps = t_o4

        t_sm = [
            s.dma("sync", s.convw[:, :, :],
                  s.w_conv.rearrange("(a p) k -> p a k", p=128)),
            s.dma("sync", s.convb[:, :].unsqueeze(2),
                  s.b_conv.rearrange("(a p) o -> p a o", p=128), slow=True),
            s.dma("sync", s.dtbias[:, :].unsqueeze(2),
                  s.b_dt.rearrange("(a p) o -> p a o", p=128), slow=True),
            s.dma("sync", s.dprm_t[:, :].unsqueeze(2),
                  s.d_prm.rearrange("(a p) o -> p a o", p=128), slow=True),
        ]
        s.t_small = t_sm
        t_al = s.dma("sync",
                     s.prepbuf[:, 144:144 + N_DT * D_STATE]
                     .rearrange("p (a n) -> p a n", n=D_STATE),
                     s.a_log.rearrange("(a p) n -> p a n", p=128))
        s.t_apos = s.s(lambda e: e.activation(
            s.apos[:, :, :].rearrange("p a n -> p (a n)"),
            s.prepbuf[:, 144:144 + N_DT * D_STATE], AF.Exp), waits=[t_al])

        s.t_ndtb = s.v(lambda e: e.tensor_scalar_mul(
            s.ndtb[:, :], s.dtbias[:, :], -1.0), waits=[t_sm[2]])
        s.t_cast = []
        for src, dst in ((s.w_inproj, s.inproj_b), (s.w_dt, s.dt_w16),
                         (s.w_outproj, s.outproj_b)):
            s.t_cast.append(s.dma("gpsimd", dst[:, :], src[:, :]))
        s.t_cast.append(s.dma("gpsimd", s.bc_b16[0:D_STATE, :],
                              s.w_bproj[:, :]))
        s.t_cast.append(s.dma("gpsimd", s.bc_b16[D_STATE:, :],
                              s.w_cproj[:, :]))

        s.t_identb = s.v(lambda e: e.tensor_copy(
            s.identb[:, :], s.identf[:, :]), waits=[s.t_ident])
        s.t_carry0 = s.g(lambda e: e.memset(s.carry[:, :, :], 0.0))
        s.t_acol0 = s.g(lambda e: e.memset(
            s.scA[:, :, :, 0:1].rearrange("p a b o -> p (a b o)"), 0.0))
        s.t_regs = s.t_meta
        self.quant_weights()
        self.pretranspose()

    def pretranspose(self):
        s = self
        specs = [
            (s.gate_q, s.gate_qT, FFN_INNER, D_MODEL, s.t_wq_g, "g"),
            (s.up_q, s.up_qT, FFN_INNER, D_MODEL, s.t_wq_u, "u"),
            (s.down_q, s.down_qT, D_MODEL, FFN_INNER, s.t_wq_d, "d"),
            (s.inproj_b, s.inprojT, 2 * D_INNER, D_MODEL, s.t_cast[0], "ip"),
            (s.dt_w16, s.dt_wT, D_INNER, D_INNER, s.t_cast[1], "dt"),
            (s.outproj_b, s.outprojT, D_MODEL, D_INNER, s.t_cast[2], "op"),
        ]
        slot = 0
        for (srcm, dstm, R, C, ready, nm) in specs:
            writes = []
            for rt in range(R // 512):
                for ct in range(C // 128):
                    sl, c8 = divmod(slot % 16, 8)
                    t_l = s.dma("sync", s.wst[:, sl, c8, :],
                                srcm[rt * 512:(rt + 1) * 512,
                                     ct * 128:(ct + 1) * 128],
                                waits=[ready,
                                       s.free.get(("wtp", slot % 16))],
                                transpose=True)
                    t_w = s.dma("sync",
                                dstm[ct * 128:(ct + 1) * 128,
                                     rt * 512:(rt + 1) * 512],
                                s.wst[:, sl, c8, :], waits=[t_l])
                    s.free[("wtp", slot % 16)] = t_w
                    writes.append(t_w)
                    slot += 1
            setattr(s, "t_wT_" + nm, writes)
        for i in range(16):
            sl, c8 = divmod(i, 8)
            s.free[("wst", sl, c8)] = s.free.get(("wtp", i))
        bcw = []
        for ct in range(16):
            t_l = s.dma("sync", s.wbc[:, ct, :],
                        s.bc_b16[0:32, ct * 128:(ct + 1) * 128],
                        waits=[s.t_cast[3], s.t_cast[4]], transpose=True)
            t_w = s.dma("sync", s.bcT[ct * 128:(ct + 1) * 128, :],
                        s.wbc[:, ct, :], waits=[t_l])
            s.free[("wbc", ct)] = t_w
            bcw.append(t_w)
        s.t_wT_bc = bcw

    def quant_weights(self):
        s = self
        specs = [("g", s.w_gate, s.gate_q, FFN_INNER, D_MODEL, 0),
                 ("u", s.w_up, s.up_q, FFN_INNER, D_MODEL, 1),
                 ("d", s.w_down, s.down_q, D_MODEL, FFN_INNER, 2)]
        for nm, src, dst, R, C, col in specs:
            nrt, npc = R // 128, C // 1024
            ntile = nrt * npc
            acc = None
            for j in range(ntile):
                r, pc = divmod(j, npc)
                slot = j % 2
                tl = s.dma("sync", s.xin[:, slot, :],
                           src[r * 128:(r + 1) * 128,
                               pc * 1024:(pc + 1) * 1024],
                           waits=[s.free.get(("xin", slot))],
                           key=("xin", slot))
                ci = j % s.NTI
                tr = s.v(lambda e, slot=slot, ci=ci: e.tensor_reduce(
                    s.st_t1[:, 0, ci:ci + 1], s.xin[:, slot, :], AX.X,
                    ALU.add, apply_absolute_value=True), waits=[tl])
                s.free[("xin", slot)] = tr
                if acc is None:
                    acc = s.v(lambda e, ci=ci: e.tensor_copy(
                        s.st_t0[:, 0, 0:1], s.st_t1[:, 0, ci:ci + 1]),
                        waits=[tr])
                else:
                    acc = s.v(lambda e, ci=ci: e.tensor_add(
                        s.st_t0[:, 0, 0:1], s.st_t0[:, 0, 0:1],
                        s.st_t1[:, 0, ci:ci + 1]), waits=[tr])
            tp = s.t(lambda e: e.matmul(s.pS0[0:1, 0:1], s.onescol[:, :],
                                        s.st_t0[:, 0, 0:1], start=True,
                                        stop=True),
                     waits=[acc, self.bankw("pS0")] + s.t_ones)
            tm = s.s(lambda e, R=R, C=C: e.mul(
                s.prepbuf[0:1, 280:281], s.pS0[0:1, 0:1], 1.0 / (R * C)),
                waits=[tp])
            self.bankset("pS0", tm)
            tc = s.v(lambda e: e.tensor_scalar_max(
                s.prepbuf[0:1, 281:282], s.prepbuf[0:1, 280:281], EPS),
                waits=[tm])
            tst = s.dma("scalar", s.swrow_d[col:col + 1, :],
                        s.prepbuf[0:1, 281:282], waits=[tc])
            trep = s.dma("scalar", s.swinv[:, col:col + 1],
                         s.swrow_d[col:col + 1, 0:1].unsqueeze(0)
                         .broadcast_to([128, 1, 1])
                         .rearrange("p a o -> p (a o)"), waits=[tst])
            setattr(s, "t_swinv_" + nm, trep)
            tsw = s.v(lambda e, col=col: e.reciprocal(
                s.swinv[:, 3 + col:4 + col], s.swinv[:, col:col + 1]),
                waits=[trep])
            t_writes = []
            for j in range(ntile):
                r, pc = divmod(j, npc)
                slot = j % 2
                sl2 = 0
                tl = s.dma("sync", s.xin[:, slot, :],
                           src[r * 128:(r + 1) * 128,
                               pc * 1024:(pc + 1) * 1024],
                           waits=[s.free.get(("xin", slot)), tsw],
                           key=("xin", slot))
                ta = s.s(lambda e, slot=slot, sl2=sl2, col=col: e.activation(
                    s.aq[:, sl2, :], s.xin[:, slot, :], AF.Copy,
                    scale=s.swinv[:, 3 + col:4 + col]),
                    waits=[tl, s.free.get(("aq", sl2))])
                s.free[("xin", slot)] = ta
                tc8 = s.s(lambda e, sl2=sl2: e.copy(
                    s.c8a[:, sl2, :], s.aq[:, sl2, :]),
                    waits=[ta, s.free.get(("c8a", sl2))])
                td = s.g(lambda e, sl2=sl2: e.tensor_sub(
                    s.dda[:, sl2, :], s.aq[:, sl2, :], s.c8a[:, sl2, :]),
                    waits=[tc8, s.free.get(("dda", sl2))])
                s.free[("aq", sl2)] = td
                tr1 = s.v(lambda e, sl2=sl2: e.scalar_tensor_tensor(
                    s.r1a[:, sl2, :], s.dda[:, sl2, :], 0.5,
                    s.c8a[:, sl2, :], ALU.is_ge, ALU.add),
                    waits=[td, s.free.get(("r1a", sl2))])
                s.free[("c8a", sl2)] = tr1
                tq = s.v(lambda e, sl2=sl2: e.scalar_tensor_tensor(
                    s.aq[:, sl2, :], s.dda[:, sl2, :], -0.5,
                    s.r1a[:, sl2, :], ALU.is_le, ALU.subtract),
                    waits=[tr1, s.free.get(("aq", sl2))])
                s.free[("dda", sl2)] = tq
                s.free[("r1a", sl2)] = tq
                tcl = s.v(lambda e, sl2=sl2: e.tensor_scalar(
                    s.xqn[:, sl2, :], s.aq[:, sl2, :], 1.0, -1.0, ALU.min,
                    ALU.max), waits=[tq, s.free.get(("xqn", sl2))])
                s.free[("aq", sl2)] = tcl
                tw = s.dma("scalar", dst[r * 128:(r + 1) * 128,
                                         pc * 1024:(pc + 1) * 1024],
                           s.xqn[:, sl2, :], waits=[tcl], key="wqw")
                s.free[("xqn", sl2)] = tw
                t_writes.append(tw)
            setattr(s, "t_wq_" + nm, t_writes)
        last = [s.t_wq_d[-1]]
        s.t_wq_g = s.t_wq_u = s.t_wq_d = last

    def main(self):
        s = self
        s.x1_ready = {}
        for k in range(N_TC):
            self.ffn_block(k, call=1)
            if STAGE in ("mamba_tok", "scan", "full"):
                self.mamba_token(k)
        if STAGE in ("scan", "full"):
            for k in range(N_TC):
                self.stage_scan(k)
            self.scan_phase(list(range(16)))
        if STAGE == "full":
            for kk in range(N_TC):
                self.post_chunk(kk)
                self.ffn_block(2 * kk, call=2)
                self.ffn_block(2 * kk + 1, call=2)

    @staticmethod
    def chunk_rows(k):
        return (0, TC_W + PFX) if k == 0 else (PFX + k * TC_W, TC_W)

    @staticmethod
    def agrow_dt(po):
        return (po // 8) * 2048 + (po % 8) * 128

    @staticmethod
    def agrow_xs(j):
        return (j // 8) * 2048 + DH + (j % 8) * 128

    def ffn_block(self, k, call):
        s = self
        if call == 1:
            r0, W = self.chunk_rows(k)
            xsrc, lnsrc, lnrow = s.x_in, s.lnq_d, s.s1row1_d
            s2row, ginq = s.s2row1_d, s.ginq1_d
            resdst = s.x1_d
            res_extra = None
        else:
            r0, W = k * 512, 512
            xsrc, lnsrc, lnrow = s.x2_d, s.ln3_d, s.s1row2_d
            s2row, ginq = s.s2row2_d, s.ginq2_d
            resdst = s.out
            res_extra = s.free.get(("x2w", k // 2))
        nti = W // 128
        par = (call * N_TC + k) % 2
        main0 = max(W - TC_W, 0)
        st = self.ln_quant(k, call, par, xsrc, r0, W, nti, lnsrc, lnrow,
                           quant=True, src_ready=res_extra)
        if W == 512:
            passes = [(0, 512)]
        else:
            passes = [(main0, 512), (main0 + 512, 512)]
            if main0:
                passes.append((0, 128))
        for (ocol, wcol) in passes:
            t_lnT = self.load_T(lnsrc, r0 + ocol, wcol,
                                waits_extra=st["t_q"])
            t_gu_last = self.ffn_gateup(k, call, par, r0, W, ocol, wcol,
                                        lnrow, t_lnT, st)
            for g_ in range(8):
                s.free[("lnT", g_)] = t_gu_last
        self.ffn_rowmax_s2(k, call, par, r0, W, nti, s2row)
        for (ocol, wcol) in passes:
            self.ffn_a2quant(k, call, par, r0, ocol, wcol, s2row, ginq)
        t_res = []
        for (ocol, wcol) in passes:
            t_res.append(self.ffn_down(k, call, par, r0, ocol, wcol, ginq,
                                       xsrc, resdst))
        if call == 1:
            s.x1_ready[k] = t_res
        else:
            s.free[("outw", k)] = t_res
        s.free[("stats", par)] = t_res[-1]

    def ln_quant(self, k, call, par, xsrc, r0, W, nti, lndst, srowdst,
                 quant, src_ready=None):
        s = self
        NI = nti
        t_stats = []
        stats_free = s.free.get(("stats", par))
        for i in range(nti):
            slot = i % 2
            tl = s.dma("scalar", s.xin[:, slot, :],
                       xsrc[r0 + i * 128:r0 + (i + 1) * 128, :],
                       waits=[s.free.get(("xin", slot)), src_ready],
                       key=("xin", slot))
            t1 = s.s(lambda e, slot=slot, i=i, par=par: e.activation(
                s.gqn[:, :], s.xin[:, slot, :], AF.Square,
                accum_out=s.st_sq[:, par, i:i + 1]),
                waits=[tl, stats_free, s.free.get("gqn")])
            t2 = s.s(lambda e, slot=slot, i=i, par=par: e.activation(
                s.gqn[:, :], s.xin[:, slot, :], AF.Copy,
                accum_out=s.st_sum[:, par, i:i + 1]), waits=[tl, t1])
            s.free["gqn"] = t2
            s.free[("xin", slot)] = t2
            t_stats += [t1, t2]
        P = par
        tb1 = s.v(lambda e: e.tensor_scalar_mul(
            s.st_m[:, P, 0:NI], s.st_sum[:, P, 0:NI], 1.0 / D_MODEL),
            waits=t_stats + [stats_free])
        tb2 = s.v(lambda e: e.tensor_mul(
            s.st_t0[:, P, 0:NI], s.st_m[:, P, 0:NI], s.st_m[:, P, 0:NI]),
            waits=[tb1])
        tb3 = s.v(lambda e: e.scalar_tensor_tensor(
            s.st_var[:, P, 0:NI], s.st_sq[:, P, 0:NI], 1.0 / D_MODEL,
            s.st_t0[:, P, 0:NI], ALU.mult, ALU.subtract), waits=[tb2])
        tb4 = s.s(lambda e: e.activation(
            s.st_t0[:, P, 0:NI], s.st_var[:, P, 0:NI], AF.Sqrt,
            bias=s.constv[:, 0:1]), waits=[tb3, s.t_eps])
        tb5 = s.v(lambda e: e.reciprocal(
            s.st_rstd[:, P, 0:NI], s.st_t0[:, P, 0:NI]), waits=[tb4])
        tb6 = s.v(lambda e: e.scalar_tensor_tensor(
            s.st_b0[:, P, 0:NI], s.st_m[:, P, 0:NI], -1.0,
            s.st_rstd[:, P, 0:NI], ALU.mult, ALU.mult), waits=[tb5])
        if not quant:
            t_q = []
            for i in range(nti):
                slot = i % 2
                sl2 = 0
                tl = s.dma("scalar", s.xin[:, slot, :],
                           xsrc[r0 + i * 128:r0 + (i + 1) * 128, :],
                           waits=[s.free.get(("xin", slot))],
                           key=("xin", slot))
                ta = s.s(lambda e, slot=slot, sl2=sl2, i=i: e.activation(
                    s.xqn[:, sl2, :], s.xin[:, slot, :], AF.Identity,
                    scale=s.st_rstd[:, P, i:i + 1],
                    bias=s.st_b0[:, P, i:i + 1]),
                    waits=[tl, tb6, s.free.get(("xqn", sl2))])
                s.free[("xin", slot)] = ta
                tw = s.dma("scalar",
                           lndst[r0 + i * 128:r0 + (i + 1) * 128, :],
                           s.xqn[:, sl2, :], waits=[ta], key="lnw")
                s.free[("xqn", sl2)] = tw
                t_q.append(tw)
            s.free[("stats", par)] = t_q
            return {"t_q": t_q, "t_srow": None}
        t_mx = []
        for i in range(nti):
            slot = i % 2
            sl2 = 0
            tl = s.dma("scalar", s.xin[:, slot, :],
                       xsrc[r0 + i * 128:r0 + (i + 1) * 128, :],
                       waits=[s.free.get(("xin", slot))],
                       key=("xin", slot))
            ta = s.s(lambda e, slot=slot, sl2=sl2, i=i: e.activation(
                s.aq[:, sl2, :], s.xin[:, slot, :], AF.Identity,
                scale=s.st_rstd[:, P, i:i + 1], bias=s.st_b0[:, P, i:i + 1]),
                waits=[tl, tb6, s.free.get(("aq", sl2))])
            s.free[("xin", slot)] = ta
            tm = s.v(lambda e, sl2=sl2, i=i: e.tensor_reduce(
                s.st_mx[:, P, i:i + 1], s.aq[:, sl2, :], AX.X, ALU.max,
                apply_absolute_value=True), waits=[ta])
            s.free[("aq", sl2)] = tm
            t_mx.append(tm)
        tb7 = s.v(lambda e: e.tensor_scalar_max(
            s.st_t0[:, P, 0:NI], s.st_mx[:, P, 0:NI], EPS), waits=t_mx)
        tb8 = s.v(lambda e: e.reciprocal(
            s.st_t1[:, P, 0:NI], s.st_t0[:, P, 0:NI]), waits=[tb7])
        tb9 = s.v(lambda e: e.tensor_scalar_mul(
            s.st_s1[:, P, 0:NI], s.st_t1[:, P, 0:NI], 127.0), waits=[tb8])
        tb10 = s.v(lambda e: e.tensor_mul(
            s.st_rs1[:, P, 0:NI], s.st_rstd[:, P, 0:NI],
            s.st_s1[:, P, 0:NI]), waits=[tb9])
        tb11 = s.v(lambda e: e.tensor_mul(
            s.st_bs1[:, P, 0:NI], s.st_b0[:, P, 0:NI], s.st_s1[:, P, 0:NI]),
            waits=[tb9])
        tb12 = s.v(lambda e: e.tensor_scalar_mul(
            s.st_s1inv[:, P, 0:NI], s.st_t0[:, P, 0:NI], 1.0 / 127.0),
            waits=[tb9])
        t_tr0 = s.t(lambda e: e.transpose(
            s.pS0[0:NI, 0:128], s.st_s1inv[:, P, 0:NI], s.identf[:, :]),
            waits=[tb12, s.t_ident, self.bankw("pS0")])
        t_cp0 = s.s(lambda e: e.copy(s.rowstage[0:NI, :],
                                     s.pS0[0:NI, 0:128]),
                    waits=[t_tr0, s.free.get("rowstage")])
        self.bankset("pS0", t_cp0)
        t_srow = s.dma("scalar",
                       srowdst[0:1, r0:r0 + W]
                       .rearrange("a (i p) -> (a i) p", p=128),
                       s.rowstage[0:NI, :], waits=[t_cp0], key="rowsw")
        s.free["rowstage"] = t_srow
        t_q = []
        for i in range(nti):
            slot = i % 2
            sl2 = 0
            tl = s.dma("scalar", s.xin[:, slot, :],
                       xsrc[r0 + i * 128:r0 + (i + 1) * 128, :],
                       waits=[s.free.get(("xin", slot))],
                       key=("xin", slot))
            ta = s.s(lambda e, slot=slot, sl2=sl2, i=i: e.activation(
                s.aq[:, sl2, :], s.xin[:, slot, :], AF.Identity,
                scale=s.st_rs1[:, P, i:i + 1], bias=s.st_bs1[:, P, i:i + 1]),
                waits=[tl, tb10, tb11, s.free.get(("aq", sl2))])
            s.free[("xin", slot)] = ta
            tc8 = s.s(lambda e, sl2=sl2: e.copy(
                s.c8a[:, sl2, :], s.aq[:, sl2, :]),
                waits=[ta, s.free.get(("c8a", sl2))])
            td = s.g(lambda e, sl2=sl2: e.tensor_sub(
                s.dda[:, sl2, :], s.aq[:, sl2, :], s.c8a[:, sl2, :]),
                waits=[tc8, s.free.get(("dda", sl2))])
            s.free[("aq", sl2)] = td
            tr1 = s.v(lambda e, sl2=sl2: e.scalar_tensor_tensor(
                s.r1a[:, sl2, :], s.dda[:, sl2, :], 0.5, s.c8a[:, sl2, :],
                ALU.is_ge, ALU.add), waits=[td, s.free.get(("r1a", sl2))])
            s.free[("c8a", sl2)] = tr1
            tq = s.v(lambda e, sl2=sl2: e.scalar_tensor_tensor(
                s.xqn[:, sl2, :], s.dda[:, sl2, :], -0.5, s.r1a[:, sl2, :],
                ALU.is_le, ALU.subtract),
                waits=[tr1, s.free.get(("xqn", sl2))])
            s.free[("dda", sl2)] = tq
            s.free[("r1a", sl2)] = tq
            tw = s.dma("scalar", lndst[r0 + i * 128:r0 + (i + 1) * 128, :],
                       s.xqn[:, sl2, :], waits=[tq], key="lnw")
            s.free[("xqn", sl2)] = tw
            t_q.append(tw)
        return {"t_q": t_q, "t_srow": t_srow}

    def load_T(self, src_d, r0, wcol, waits_extra=()):
        s = self
        res = []
        for g_ in range(8):
            res.append(s.dma("sync", s.lnT[:, g_, 0:wcol],
                             src_d[r0:r0 + wcol, g_ * 128:(g_ + 1) * 128],
                             waits=[s.free.get(("lnT", g_))] +
                             list(waits_extra), transpose=True,
                             key="lnT"))
        return res

    def ffn_gateup(self, k, call, par, r0, W, ocol, wcol, lnrow,
                   t_lnT, st):
        s = self
        main0 = W - TC_W
        tsi = 2 if wcol == 128 else (0 if ocol == main0 else 1)
        msl = [0, 512, 1024][tsi]
        t_srb = s.dma("scalar", s.srowb[:, 0:wcol],
                      lnrow[0:1, r0 + ocol:r0 + ocol + wcol].unsqueeze(0)
                      .broadcast_to([128, 1, wcol])
                      .rearrange("p a w -> p (a w)"),
                      waits=[st["t_srow"], s.free.get("srowb")],
                      key="srb")
        srb2 = (s.srowb[:, 0:wcol].unsqueeze(1)
                .broadcast_to([128, 2, wcol]))
        first_pp = True
        t_mx = None
        t_mm_all = None
        t_gw_list = []
        for pr in range(8):
            t_g8 = s.dma(
                "sync", s.wst[:, 0, :, :],
                s.gate_qT[:, pr * 512:(pr + 1) * 512]
                .rearrange("(c p) w -> p c w", p=128),
                waits=[[s.free.get(("wst", 0, c)) for c in range(8)],
                       s.t_wT_g], key="wst")
            t_u8 = s.dma(
                "sync", s.wst[:, 1, :, :],
                s.up_qT[:, pr * 512:(pr + 1) * 512]
                .rearrange("(c p) w -> p c w", p=128),
                waits=[[s.free.get(("wst", 1, c)) for c in range(8)],
                       s.t_wT_u], key="wst")
            tgld, tuld = [t_g8], [t_u8]
            t_pr_last = None
            for pp in range(2):
                t_gmm = t_umm = None
                for sub in range(2):
                    po4 = pp * 2 + sub
                    for c in range(8):
                        t_gmm = s.t(lambda e, c=c, po4=po4, sub=sub, w=wcol:
                                    e.matmul(
                                        s.pTT[:, sub * 512:sub * 512 + w],
                                        s.wst[:, 0, c,
                                              po4 * 128:(po4 + 1) * 128],
                                        s.lnT[:, c, 0:w],
                                        start=(c == 0), stop=(c == 7)),
                                    waits=[tgld[-1], tuld[-1], t_lnT[-1],
                                           self.bankw("pTT")
                                           if (c == 0 and sub == 0)
                                           else None])
                for sub in range(2):
                    po4 = pp * 2 + sub
                    for c in range(8):
                        t_umm = s.t(lambda e, c=c, po4=po4, sub=sub, w=wcol:
                                    e.matmul(
                                        s.pUU[:, sub * 512:sub * 512 + w],
                                        s.wst[:, 1, c,
                                              po4 * 128:(po4 + 1) * 128],
                                        s.lnT[:, c, 0:w],
                                        start=(c == 0), stop=(c == 7)),
                                    waits=[tuld[-1],
                                           self.bankw("pUU")
                                           if (c == 0 and sub == 0)
                                           else None])
                t_pr_last = t_umm
                po = pr * 4 + pp * 2
                pTv = (s.pTT[:, :].rearrange("p (a b) -> p a b", a=2)
                       [:, :, 0:wcol])
                pUv = (s.pUU[:, :].rearrange("p (a b) -> p a b", a=2)
                       [:, :, 0:wcol])
                grv = (s.graw[:, :].rearrange("p (a b) -> p a b", a=2)
                       [:, :, 0:wcol])
                gsv = (s.gsil[:, :].rearrange("p (a b) -> p a b", a=2)
                       [:, :, 0:wcol])
                gav = (s.gabs[:, :].rearrange("p (a b) -> p a b", a=2)
                       [:, :, 0:wcol])
                t_gt = s.v(lambda e, pTv=pTv, grv=grv, srb2=srb2:
                           e.scalar_tensor_tensor(
                               grv, pTv, s.swinv[:, 0:1], srb2,
                               ALU.mult, ALU.mult),
                           waits=[t_gmm, t_srb, s.t_swinv_g,
                                  s.free.get("graw")])
                self.bankset("pTT", t_gt)
                t_gs = s.s(lambda e, grv=grv, gsv=gsv: e.activation(
                    gsv, grv, AF.Silu), waits=[t_gt, s.free.get("gsil")])
                t_gr = s.v(lambda e, gsv=gsv, pUv=pUv, grv=grv:
                           e.tensor_tensor(grv, gsv, pUv, ALU.mult),
                           waits=[t_gs, t_umm])
                self.bankset("pUU", t_gr)
                s.free["gsil"] = t_gr
                t_gw = s.dma("sync",
                             s.graw_d[po * 128:(po + 2) * 128,
                                      ocol:ocol + wcol]
                             .rearrange("(a q) w -> q a w", a=2),
                             grv, waits=[t_gr, s.free.get("a2_last_read")],
                             key="grawd")
                t_gw_list.append(t_gw)
                t_ab = s.s(lambda e, grv=grv, gav=gav: e.activation(
                    gav, grv, AF.Abs), waits=[t_gr, s.free.get("gabs")])
                s.free["graw"] = [t_gw, t_ab]
                mview = s.macc[:, :, msl:msl + wcol]
                if first_pp:
                    t_mx = s.v(lambda e, gav=gav, mview=mview:
                               e.tensor_copy(mview, gav),
                               waits=[t_ab, s.free.get(("macc", tsi))])
                    first_pp = False
                else:
                    t_mx = s.v(lambda e, gav=gav, mview=mview:
                               e.tensor_max(mview, mview, gav),
                               waits=[t_ab])
                s.free["gabs"] = t_mx
            for c in range(8):
                s.free[("wst", 0, c)] = t_pr_last
                s.free[("wst", 1, c)] = t_pr_last
            t_mm_all = t_pr_last
        s.free[("maccdone", call, k, tsi)] = t_mx
        s.free["srowb"] = t_mx
        s.free.setdefault(("grawd", call, k), []).extend(t_gw_list)
        return t_mm_all

    def ffn_rowmax_s2(self, k, call, par, r0, W, nti, s2row):
        s = self
        P = par
        main0 = W - TC_W
        NI = nti
        t_red = []
        for i in range(nti):
            ocol = i * 128
            tsi = 2 if (main0 and ocol < main0) else \
                (0 if ocol < main0 + 512 else 1)
            base = 0 if tsi == 2 else (main0 if tsi == 0 else main0 + 512)
            off = [0, 512, 1024][tsi] + (ocol - base)
            for half in range(2):
                t_tr = s.t(lambda e, off=off, half=half: e.transpose(
                    s.pS0[:, 0:128], s.macc[:, half, off:off + 128],
                    s.identf[:, :]),
                    waits=[s.free[("maccdone", call, k, tsi)], s.t_ident,
                           self.bankw("pS0")])
                dst = (s.st_mx2 if half == 0 else s.st_f)
                t_rd = s.v(lambda e, i=i, dst=dst: e.tensor_reduce(
                    dst[:, P, i:i + 1], s.pS0[:, 0:128], AX.X, ALU.max),
                    waits=[t_tr])
                self.bankset("pS0", t_rd)
                t_red.append(t_rd)
        t_cmb = s.v(lambda e: e.tensor_max(
            s.st_mx2[:, P, 0:NI], s.st_mx2[:, P, 0:NI], s.st_f[:, P, 0:NI]),
            waits=t_red)
        for tsi in (0, 1, 2):
            if ("maccdone", call, k, tsi) in s.free:
                s.free[("macc", tsi)] = t_cmb
        t1 = s.v(lambda e: e.tensor_scalar(
            s.st_f[:, P, 0:NI], s.st_s1inv[:, P, 0:NI], s.swinv[:, 1:2],
            None, ALU.mult), waits=[t_cmb, s.t_swinv_u])
        t2 = s.v(lambda e: e.tensor_mul(
            s.st_t0[:, P, 0:NI], s.st_mx2[:, P, 0:NI], s.st_f[:, P, 0:NI]),
            waits=[t1])
        t3 = s.v(lambda e: e.tensor_scalar_max(
            s.st_t1[:, P, 0:NI], s.st_t0[:, P, 0:NI], EPS), waits=[t2])
        t4 = s.v(lambda e: e.reciprocal(
            s.st_t0[:, P, 0:NI], s.st_t1[:, P, 0:NI]), waits=[t3])
        t5 = s.v(lambda e: e.tensor_scalar_mul(
            s.st_s2a[:, P, 0:NI], s.st_t0[:, P, 0:NI], 127.0), waits=[t4])
        t6 = s.v(lambda e: e.tensor_mul(
            s.st_s2a[:, P, 0:NI], s.st_s2a[:, P, 0:NI], s.st_f[:, P, 0:NI]),
            waits=[t5])
        t_tr7 = s.t(lambda e: e.transpose(
            s.pS0[0:NI, 0:128], s.st_s2a[:, P, 0:NI], s.identf[:, :]),
            waits=[t6, s.t_ident, self.bankw("pS0")])
        t_cp7 = s.s(lambda e: e.copy(s.rowstage[0:NI, :],
                                     s.pS0[0:NI, 0:128]),
                    waits=[t_tr7, s.free.get("rowstage")])
        self.bankset("pS0", t_cp7)
        t7 = s.dma("scalar",
                   s2row[0:1, r0:r0 + W]
                   .rearrange("a (i p) -> (a i) p", p=128),
                   s.rowstage[0:NI, :], waits=[t_cp7], key="rowsw")
        s.free["rowstage"] = t7
        s.free[("s2row", call, k)] = t7
        t8 = s.v(lambda e: e.tensor_scalar_mul(
            s.st_t0[:, P, 0:NI], s.st_t1[:, P, 0:NI], 1.0 / 127.0),
            waits=[t5])
        t9 = s.v(lambda e: e.tensor_scalar(
            s.st_dq[:, P, 0:NI], s.st_t0[:, P, 0:NI], s.swinv[:, 2:3], 0.5,
            ALU.mult, ALU.mult), waits=[t8, s.t_swinv_d])
        s.free[("dq", call, k)] = t9

    def ffn_a2quant(self, k, call, par, r0, ocol, wcol, s2row, ginq):
        s = self
        t_s2b = s.dma("scalar", s.srowb[:, 0:wcol],
                      s2row[0:1, r0 + ocol:r0 + ocol + wcol].unsqueeze(0)
                      .broadcast_to([128, 1, wcol])
                      .rearrange("p a w -> p (a w)"),
                      waits=[s.free[("s2row", call, k)],
                             s.free.get("srowb")],
                      key="s2b")
        s2b2 = (s.srowb[:, 0:wcol].unsqueeze(1)
                .broadcast_to([128, 2, wcol]))
        t_wr = None
        t_ld_list = []
        t_wr_list = []

        def v2(buf):
            return buf.rearrange("p (a b) -> p a b", a=2)[:, :, 0:wcol]
        for pp in range(16):
            po = pp * 2
            t_ld = s.dma("scalar", v2(s.gabs[:, :]),
                         s.graw_d[po * 128:(po + 2) * 128, ocol:ocol + wcol]
                         .rearrange("(a q) w -> q a w", a=2),
                         waits=[s.free.get("gabs"),
                                s.free[("grawd", call, k)][-1]],
                         key="a2ld")
            t_ld_list.append(t_ld)
            t_a2 = s.v(lambda e, w=wcol: e.tensor_tensor(
                v2(s.aq[:, 0, :]), v2(s.gabs[:, :]), s2b2, ALU.mult),
                waits=[t_ld, t_s2b, s.free.get(("aq", 0))])
            s.free["gabs"] = t_a2
            t_c8 = s.s(lambda e: e.copy(v2(s.c8a[:, 0, :]),
                                        v2(s.aq[:, 0, :])),
                       waits=[t_a2, s.free.get(("c8a", 0))])
            t_dd = s.g(lambda e: e.tensor_sub(v2(s.dda[:, 0, :]),
                                              v2(s.aq[:, 0, :]),
                                              v2(s.c8a[:, 0, :])),
                       waits=[t_c8, s.free.get(("dda", 0))])
            s.free[("aq", 0)] = t_dd
            t_r1 = s.v(lambda e: e.scalar_tensor_tensor(
                v2(s.r1a[:, 0, :]), v2(s.dda[:, 0, :]), 0.5,
                v2(s.c8a[:, 0, :]), ALU.is_ge, ALU.add),
                waits=[t_dd, s.free.get(("r1a", 0))])
            s.free[("c8a", 0)] = t_r1
            t_gq = s.v(lambda e: e.scalar_tensor_tensor(
                v2(s.xqn[:, 0, :]), v2(s.dda[:, 0, :]), -0.5,
                v2(s.r1a[:, 0, :]), ALU.is_le,
                ALU.subtract), waits=[t_r1, s.free.get(("xqn", 0))])
            s.free[("dda", 0)] = t_gq
            s.free[("r1a", 0)] = t_gq
            t_wr = s.dma("scalar",
                         ginq[po * 128:(po + 2) * 128,
                              r0 + ocol:r0 + ocol + wcol]
                         .rearrange("(a q) w -> q a w", a=2),
                         v2(s.xqn[:, 0, :]), waits=[t_gq], key="ginqw")
            s.free[("xqn", 0)] = t_wr
            t_wr_list.append(t_wr)
        s.free[("ginq", call, k, ocol)] = t_wr_list
        s.free.setdefault(("ginqa", call, k), []).append(t_wr)
        s.free["srowb"] = t_wr
        s.free["a2_last_read"] = t_ld_list

    def ffn_down(self, k, call, par, r0, ocol, wcol, ginq, xsrc, resdst):
        s = self
        P = par
        ntp = wcol // 128
        dbanks = ["pT0", "pT1", "pT2", "pT3"]
        t_ev_h0 = {}
        t_w_list = []
        for half in range(2):
            t_mm_last = None
            t_mm = {tp: None for tp in range(ntp)}
            t_ev_prev = [self.bankw(dbanks[tp % 4]) for tp in range(ntp)]
            for ch in range(2):
                t_wd = [s.dma(
                    "sync",
                    s.wst[:, :, :, :].rearrange("p a b w -> p (a b) w"),
                    s.down_qT[ch * 2048:(ch + 1) * 2048,
                              half * 512:(half + 1) * 512]
                    .rearrange("(c p) w -> p c w", p=128),
                    waits=[[s.free.get(("wst", c // 8, c % 8))
                            for c in range(16)], s.t_wT_d], key="wst")]
                for tpp in range((ntp + 1) // 2):
                    tps = [tp for tp in (2 * tpp, 2 * tpp + 1) if tp < ntp]
                    wblk = 128 * len(tps)
                    t_gb = s.dma("sync", s.gst[:, :, 0:wblk],
                                 ginq[ch * 2048:(ch + 1) * 2048,
                                      r0 + ocol + tps[0] * 128:
                                      r0 + ocol + tps[0] * 128 + wblk]
                                 .rearrange("(c p) t -> p c t", p=128),
                                 waits=[s.free[("ginqa", call, k)][-1],
                                        s.free.get("gst")], key="gst")
                    for tp in tps:
                        bn = dbanks[tp % 4]
                        bank = self.banks[bn]
                        tmm = None
                        for c16 in range(16):
                            tmm = s.t(lambda e, c16=c16, tp=tp, t0=tps[0],
                                      bank=bank, ch=ch:
                                      e.matmul(bank[:, 0:512],
                                               s.gst[:, c16,
                                                     (tp - t0) * 128:
                                                     (tp - t0 + 1) * 128],
                                               s.wst[:, c16 // 8,
                                                     c16 % 8, :],
                                               start=(ch == 0 and c16 == 0),
                                               stop=(ch == 1 and c16 == 15)),
                                      waits=([t_gb, t_wd[-1]] +
                                             (t_ev_prev if (ch == 0 and
                                                            c16 == 0)
                                              else [])))
                        t_mm[tp] = tmm
                        t_mm_last = tmm
                    s.free["gst"] = t_mm_last
                for c16 in range(16):
                    s.free[("wst", c16 // 8, c16 % 8)] = t_mm_last
            for tp in range(ntp):
                if True:
                    tmm = t_mm[tp]
                    bn = dbanks[tp % 4]
                    bank = self.banks[bn]
                    i_tile = (ocol + tp * 128) // 128
                    slx = tp % 2
                    t_xl = s.dma("scalar",
                                 s.xres[:, slx, :],
                                 xsrc[r0 + ocol + tp * 128:
                                      r0 + ocol + (tp + 1) * 128,
                                      half * 512:(half + 1) * 512],
                                 waits=[s.free.get(("xres", slx))],
                                 key=("xres", slx))
                    t_ev = s.v(lambda e, bank=bank, slx=slx, half=half,
                               i_tile=i_tile: e.scalar_tensor_tensor(
                                   s.x1t[:, slx, :],
                                   bank[:, 0:512],
                                   s.st_dq[:, P, i_tile:i_tile + 1],
                                   s.xres[:, slx, :],
                                   ALU.mult, ALU.add),
                               waits=[tmm, t_xl,
                                      s.free[("dq", call, k)],
                                      s.free.get(("x1t", slx)),
                                      s.free.get("ipst")])
                    self.bankset(bn, t_ev)
                    s.free[("xres", slx)] = t_ev
                    t_w = s.dma("scalar",
                                resdst[r0 + ocol + tp * 128:
                                       r0 + ocol + (tp + 1) * 128,
                                       half * 512:(half + 1) * 512],
                                s.x1t[:, slx, :],
                                waits=[t_ev], key=("resw", slx))
                    s.free[("x1t", slx)] = t_w
                    t_w_list.append(t_w)
        return t_w_list

    def mamba_token(self, k):
        s = self
        r0, W = self.chunk_rows(k)
        nti = W // 128
        main0 = W - TC_W
        par = (2 * N_TC + k) % 2
        st = self.ln_quant(k, 10 + k, par, s.x1_d, r0, W, nti, s.ln2_d,
                           None, quant=False,
                           src_ready=s.x1_ready[k])
        tsl = self.ffn_tsl(W)
        t_conv_done = self.inproj_conv(k, r0, W, main0, tsl, st)
        t_ag_writes = []
        for h in range(2):
            t_ag_writes += self.dt_bc_half(k, h, t_conv_done[2])
        t_ag_writes += t_conv_done[2]
        cc = self.ccq[k % 2]
        extra = [Ticket(cc, cc.n)] if k >= 2 else []
        t_cc = self.em.emit(
            "gpsimd",
            lambda e, k=k: e.collective_compute(
                "AllGather", ALU.bypass,
                replica_groups=[[0, 1], [2, 3], [4, 5], [6, 7]],
                ins=[s.ag_in[k].opt()], outs=[s.ag_out[k].opt()]),
            waits=t_ag_writes + extra, inc=cc)
        s.free[("ag", k)] = t_cc

    @staticmethod
    def ffn_tsl(W):
        main0 = W - TC_W
        tsl = [(main0, 512), (main0 + 512, 512)]
        if main0:
            tsl = [(0, 128)] + tsl
        return tsl

    def inproj_conv(self, k, r0, W, main0, tsl, st):
        s = self
        t_xs_writes = []
        t_half = {0: [], 1: []}
        t_xpw = []
        for (oc, wc) in tsl:
            t_lnT = self.load_T(s.ln2_d, r0 + oc, wc, waits_extra=st["t_q"])
            t_pass_last = None
            for pr in range(8):
                tld = [s.dma(
                    "sync", s.wst[:, 0, :, :],
                    s.inprojT[:, pr * 512:(pr + 1) * 512]
                    .rearrange("(c p) w -> p c w", p=128),
                    waits=[[s.free.get(("wst", 0, c)) for c in range(8)],
                           s.t_wT_ip], key="wst")]
                for pp in range(2):
                    j0 = pr * 4 + pp * 2
                    pair = [s.pTT, s.pUU][pp]
                    bpair = ["pTT", "pUU"][pp]
                    tmm = None
                    for sub in range(2):
                        po4 = pp * 2 + sub
                        for c in range(8):
                            tmm = s.t(lambda e, c=c, po4=po4, sub=sub, wc=wc,
                                      pair=pair: e.matmul(
                                          pair[:, sub * 512:sub * 512 + wc],
                                          s.wst[:, 0, c,
                                                po4 * 128:(po4 + 1) * 128],
                                          s.lnT[:, c, 0:wc],
                                          start=(c == 0), stop=(c == 7)),
                                      waits=[tld[-1], t_lnT[-1],
                                             self.bankw(bpair)
                                             if (c == 0 and sub == 0)
                                             else None])
                    t_pass_last = tmm
                    if j0 >= 16 and main0 and oc == 0:
                        self.bankset(bpair, tmm)
                        continue
                    pv = (pair[:, :].rearrange("p (a b) -> p a b", a=2)
                          [:, :, 0:wc])
                    t_cp = s.s(lambda e, pv=pv, wc=wc: e.copy(
                        s.x1t[:, :, 0:wc], pv),
                        waits=[tmm, s.free.get("ipst"),
                               s.free.get(("x1t", 0)),
                               s.free.get(("x1t", 1))])
                    self.bankset(bpair, t_cp)
                    if j0 < 16:
                        dst = (s.xp_d[j0 * 128:(j0 + 2) * 128,
                                      r0 + oc:r0 + oc + wc]
                               .rearrange("(a q) w -> q a w", a=2))
                    else:
                        zc = k * TC_W + (oc - main0)
                        dst = (s.z_d[(j0 - 16) * 128:(j0 - 14) * 128,
                                     zc:zc + wc]
                               .rearrange("(a q) w -> q a w", a=2))
                    t_wr = s.dma("sync", dst, s.x1t[:, :, 0:wc],
                                 waits=[t_cp], key="xpzw")
                    s.free["ipst"] = t_wr
                    t_xpw.append(t_wr)
                    if j0 >= 16:
                        s.free[("zw", k)] = t_xpw
                for c in range(8):
                    s.free[("wst", 0, c)] = t_pass_last
            for g_ in range(8):
                s.free[("lnT", g_)] = t_pass_last
        for j in range(16):
            self.conv_etile(k, j, r0, main0, t_xpw, t_xs_writes, t_half)
        return {0: t_half[0], 1: t_half[1], 2: t_xs_writes}

    def conv_etile(self, k, j, r0, main0, t_xpw, t_xs_writes, t_half):
        s = self
        for h in range(2):
            slc = 0
            t0 = r0 + main0 + h * 512
            t_ld = s.dma("sync", s.cxp[:, slc, :],
                         s.xp_d[j * 128:(j + 1) * 128, t0 - 3:t0 + 512],
                         waits=[s.free.get(("cxp", slc)), t_xpw[-1]],
                         key="cxp")
            t_g0 = s.s(lambda e, j=j, slc=slc: e.activation(
                s.cg0[:, 0:512], s.cxp[:, slc, 0:512],
                AF.Identity, scale=s.convw[:, j, 0:1]),
                waits=[t_ld, s.t_small[0], s.free.get("cg0")])
            t_p1 = s.s(lambda e, j=j, slc=slc: e.activation(
                s.cp1[:, 0:512], s.cxp[:, slc, 1:513],
                AF.Identity, scale=s.convw[:, j, 1:2]),
                waits=[t_ld, s.free.get("cp1")])
            t_m1 = s.g(lambda e: e.tensor_add(
                s.cm[:, 0:512], s.cg0[:, 0:512], s.cp1[:, 0:512]),
                waits=[t_g0, t_p1, s.free.get("cm")])
            s.free["cg0"] = t_m1
            s.free["cp1"] = t_m1
            t_m2 = s.v(lambda e, j=j, slc=slc: e.scalar_tensor_tensor(
                s.cp1[:, 0:512], s.cxp[:, slc, 2:514],
                s.convw[:, j, 2:3], s.cm[:, 0:512], ALU.mult, ALU.add),
                waits=[t_m1])
            s.free["cm"] = t_m2
            t_m3 = s.v(lambda e, j=j, slc=slc: e.scalar_tensor_tensor(
                s.cm[:, 0:512], s.cxp[:, slc, 3:515],
                s.convw[:, j, 3:4], s.cp1[:, 0:512], ALU.mult, ALU.add),
                waits=[t_m2, s.free.get("cm")])
            s.free["cp1"] = t_m3
            s.free[("cxp", slc)] = t_m3
            t_si = s.s(lambda e, j=j: e.activation(
                s.xsst[:, 0:512], s.cm[:, 0:512], AF.Silu,
                bias=s.convb[:, j:j + 1]),
                waits=[t_m3, s.t_small[1], s.free.get("xsst")])
            s.free["cm"] = t_si
            rxs = self.agrow_xs(j)
            t_xw = s.dma("sync",
                         s.ag_in[k][rxs:rxs + 128,
                                    h * 512:(h + 1) * 512],
                         s.xsst[:, 0:512], waits=[t_si], key="agx")
            s.free["xsst"] = t_xw
            t_half[h].append(t_xw)
            t_xs_writes.append(t_xw)

    def dt_bc_half(self, k, h, t_xsb):
        s = self
        t_writes = []
        t_fill = []
        for c in range(16):
            rxs = self.agrow_xs(c)
            t_fill.append(s.dma(
                "gpsimd", s.xsb[:, c, :],
                s.ag_in[k][rxs:rxs + 128,
                           h * 512:(h + 1) * 512],
                waits=[t_xsb[-1], s.free.get(("xsbfree", c))],
                key="xsb"))
        t_xsb = t_fill
        for pr in range(4):
            tld = [s.dma(
                "sync", s.wst[:, :, :, :].rearrange("p a b w -> p (a b) w"),
                s.dt_wT[:, pr * 512:(pr + 1) * 512]
                .rearrange("(c p) w -> p c w", p=128),
                waits=[[s.free.get(("wst", c // 8, c % 8))
                        for c in range(16)], s.t_wT_dt], key="wst")]
            t_pr = None
            for po4 in range(4):
                po = pr * 4 + po4
                bn = ["pT0", "pT1"][po % 2]
                bank = self.banks[bn]
                tmm = None
                for c in range(16):
                    sl2, c8 = divmod(c, 8)
                    tmm = s.t(lambda e, c=c, sl2=sl2, c8=c8, po4=po4,
                              bank=bank: e.matmul(
                                  bank[:, 0:512],
                                  s.wst[:, sl2, c8, po4 * 128:(po4 + 1) * 128],
                                  s.xsb[:, c, :],
                                  start=(c == 0), stop=(c == 15)),
                              waits=[tld[-1], t_xsb[-1],
                                     self.bankw(bn) if c == 0 else None])
                t_pr = tmm
                sl = 0
                t_sg = s.s(lambda e, bank=bank, sl=sl, po=po: e.activation(
                    s.sgt[:, sl, :], bank[:, 0:512], AF.Sigmoid, scale=-1.0,
                    bias=s.ndtb[:, po:po + 1]),
                    waits=[tmm, s.t_ndtb, s.free.get(("sgt", sl))])
                self.bankset(bn, t_sg)
                t_ln = s.s(lambda e, sl=sl: e.activation(
                    s.sgt[:, sl, :], s.sgt[:, sl, :], AF.Ln),
                    waits=[t_sg])
                rdt = self.agrow_dt(po)
                t_dw = s.dma("sync",
                             s.ag_in[k][rdt:rdt + 128,
                                        h * 512:(h + 1) * 512],
                             s.sgt[:, sl, :], waits=[t_ln], key="agd")
                s.free[("sgt", sl)] = t_dw
                t_writes.append(t_dw)
            for c in range(16):
                sl2, c8 = divmod(c, 8)
                s.free[("wst", sl2, c8)] = t_pr
        t_dt_last = t_pr
        tldb = [s.dma(
            "sync", s.wbc[:, :, :],
            s.bcT[:, :].rearrange("(c p) n -> p c n", p=128),
            waits=[[s.free.get(("wbc", c)) for c in range(16)],
                   s.t_wT_bc], key="wbc")]
        tmm = None
        for c in range(16):
            tmm = s.t(lambda e, c=c: e.matmul(
                s.pS0[0:32, 0:512], s.wbc[:, c, :], s.xsb[:, c, :],
                start=(c == 0), stop=(c == 15)),
                waits=[tldb[-1], t_xsb[-1],
                       self.bankw("pS0") if c == 0 else None])
        for c in range(16):
            s.free[("wbc", c)] = tmm
        for j in range(16):
            s.free[("xsbfree", j)] = [t_dt_last, tmm]
        t_bc0 = s.s(lambda e: e.activation(
            s.bcst[0:32, :], s.pS0[0:32, 0:512], AF.Copy, scale=-1.0),
            waits=[tmm, s.free.get("bcst")])
        t_bc = s.s(lambda e: e.copy(s.bcst[0:16, :], s.pS0[0:16, 0:512]),
                   waits=[tmm, t_bc0])
        self.bankset("pS0", t_bc)
        t_bw = s.dma("sync",
                     s.ag_in[k][2 * DH + D_INNER:2 * DH + D_INNER + 32,
                                h * 512:(h + 1) * 512],
                     s.bcst[0:32, :], waits=[t_bc], key="agb")
        s.free["bcst"] = t_bw
        t_writes.append(t_bw)
        return t_writes

    def stage_scan(self, k):
        s = self
        t_ag = s.free[("ag", k)]
        for contrib in range(2):
            def fcp(e, off, k=k, contrib=contrib):
                v = s.ag_out[k][contrib * s.AGR:
                                contrib * s.AGR + s.AGR, :]
                return e.dma_start(
                    out=s.sc_d[k][contrib * 2048:
                                  (contrib + 1) * 2048, :],
                    in_=v[bass.ds(off, 2048), :])
            s.free[("scp", k, contrib)] = self.dyn_dma(
                fcp, reg_src=2, waits=[t_ag, s.t_meta])
            base = contrib * s.AGR
            s.free[("bc16", k, contrib)] = s.dma(
                "gpsimd", s.bc16_d[k][contrib * 32:contrib * 32 + 32, :],
                s.ag_out[k][base + 2 * D_INNER:
                            base + 2 * D_INNER + 32, :],
                waits=[t_ag], key="bc16")

    def scan_phase(self, cc_list):
        if SCAN_SERIAL:
            real = {n: getattr(self, n) for n in ("v", "s", "g", "t", "dma")}
            last = [None]

            def mk(f):
                def w(*a, **kw):
                    kw["waits"] = (list(kw.get("waits", ())) +
                                   ([last[0]] if last[0] is not None else []))
                    t = f(*a, **kw)
                    last[0] = t
                    return t
                return w
            for n, f in real.items():
                setattr(self, n, mk(f))
            try:
                self._scan_body(cc_list)
            finally:
                for n, f in real.items():
                    setattr(self, n, f)
        else:
            self._scan_body(cc_list)

    def _scan_body(self, cc_list):
        s = self
        for cc in cc_list:
            g, hh = divmod(cc, 2)
            k, contrib = g % 4, g // 4
            base = contrib * s.AGR
            cols = (hh * 512, hh * 512 + 512)
            t_ag = s.free[("ag", k)]
            t_cp = s.free[("scp", k, contrib)]
            t_bc16 = s.free[("bc16", k, contrib)]
            t_bf = s.dma("sync", s.bfull[:, :, :],
                         s.bc16_d[k][contrib * 32:contrib * 32 + D_STATE,
                                     cols[0]:cols[1]]
                         .unsqueeze(0).broadcast_to([128, D_STATE, 512]),
                         waits=[t_bc16, s.free.get("bfull")], key="bfl")
            t_cf = s.dma("sync", s.cfull[:, :, :],
                         s.bc16_d[k][contrib * 32 + D_STATE:
                                     contrib * 32 + 2 * D_STATE,
                                     cols[0]:cols[1]]
                         .unsqueeze(0).broadcast_to([128, D_STATE, 512]),
                         waits=[t_bc16, s.free.get("cfull")], key="cfl")
            bf_readers, cf_readers = [], []
            for dtile in range(N_DT):
                slot = dtile % 2
                r0d = contrib * 2048 + dtile * 128
                t_ld = s.dma("sync", s.dtna[:, slot, :],
                             s.sc_d[k][r0d:r0d + 128, cols[0]:cols[1]],
                             waits=[t_cp, s.free.get(("dtna", slot))],
                             key=("dtna", slot))
                t_lx = s.dma("gpsimd", s.ua[:, slot, :],
                             s.sc_d[k][r0d + DH:r0d + DH + 128,
                                       cols[0]:cols[1]],
                             waits=[t_cp, s.free.get(("ua", slot))],
                             key=("uax", slot))
                t_u = s.v(lambda e, slot=slot: e.tensor_mul(
                    s.ua[:, slot, :], s.dtna[:, slot, :],
                    s.ua[:, slot, :]),
                    waits=[t_ld, t_lx])
                dt_readers = [t_u]
                ua_readers = []
                uab = (s.ua[:, slot, :].unsqueeze(1)
                       .broadcast_to([128, NG, 512]))
                ybn = ["pY0", "pY1"][dtile % 2]
                ybank = self.banks[ybn]
                t_nred = None
                for grp in range(NGRP):
                    n0 = grp * NG
                    gsl = grp % 2
                    t_as = []
                    for nn in range(NG):
                        t_as.append(s.s(
                            lambda e, nn=nn, gsl=gsl, slot=slot,
                            dtile=dtile, n0=n0:
                            e.activation(
                                s.scA[:, gsl, nn, 1:513],
                                s.dtna[:, slot, :], AF.Exp,
                                scale=s.apos[:, dtile, n0 + nn:n0 + nn + 1]),
                            waits=[t_ld, s.t_apos, s.t_acol0,
                                   s.free.get(("scAf", gsl))]))
                    dt_readers += t_as
                    t_cj = s.s(lambda e, dtile=dtile, n0=n0:
                               e.copy(
                                   s.scB[:, :, 0:1]
                                   .rearrange("p a o -> p (a o)"),
                                   s.carry[:, dtile, n0:n0 + NG]),
                               waits=[s.t_carry0,
                                      s.free.get("scBf"),
                                      s.free.get(("carryx", dtile, grp))])
                    t_b = s.v(lambda e, n0=n0, uab=uab:
                              e.tensor_tensor(
                                  s.scB[:, :, 1:513], uab,
                                  s.bfull[:, n0:n0 + NG, :], ALU.mult),
                              waits=[t_u, t_bf])
                    ua_readers.append(t_b)
                    bf_readers.append(t_b)
                    t_sc = s.v(lambda e, gsl=gsl: e.tensor_tensor_scan(
                        s.scH[:, :, :].rearrange("p a o -> p (a o)"),
                        s.scA[:, gsl, :, :].rearrange("p a o -> p (a o)"),
                        s.scB[:, :, :].rearrange("p a o -> p (a o)"),
                        0.0, ALU.mult, ALU.add),
                        waits=t_as + [t_cj, t_b, s.free.get("scHf")])
                    s.free[("scAf", gsl)] = t_sc
                    s.free["scBf"] = t_sc
                    t_cx = s.s(lambda e, dtile=dtile, n0=n0: e.copy(
                        s.carry[:, dtile, n0:n0 + NG],
                        s.scH[:, :, 512:513]
                        .rearrange("p a o -> p (a o)")),
                        waits=[t_sc])
                    s.free[("carryx", dtile, grp)] = t_cx
                    t_hc = s.v(lambda e, n0=n0, gsl=gsl: e.tensor_tensor(
                        s.hcall[:, gsl, :, :], s.scH[:, :, 1:513],
                        s.cfull[:, n0:n0 + NG, :], ALU.mult),
                        waits=[t_cf, t_sc, s.free.get(("hcf", gsl))])
                    cf_readers.append(t_hc)
                    s.free["scHf"] = [t_hc, t_cx]
                    for nn in range(NG):
                        t_nred = s.t(
                            lambda e, nn=nn, gsl=gsl, grp=grp, ybank=ybank:
                            e.matmul(
                                ybank, s.identb[:, :],
                                s.hcall[:, gsl, nn, :],
                                start=(grp == 0 and nn == 0),
                                stop=(grp == NGRP - 1 and nn == NG - 1)),
                            waits=[t_hc, s.t_identb] +
                            ([self.bankw(ybn)] if grp == 0 and nn == 0
                             else []))
                    s.free[("hcf", gsl)] = t_nred
                s.free[("dtna", slot)] = dt_readers
                s.free[("ua", slot)] = ua_readers
                t_ye = s.s(lambda e, ybank=ybank: e.copy(
                    s.yst[:, :], ybank), waits=[t_nred, s.free.get("ystf")])
                self.bankset(ybn, t_ye)
                t_yw = s.dma("sync",
                             s.yag_in[g][dtile * 128:(dtile + 1) * 128,
                                         cols[0]:cols[1]],
                             s.yst[:, :], waits=[t_ye], key="yagw")
                s.free["ystf"] = t_yw
                s.free.setdefault(("ywr", g), []).append(t_yw)
            s.free["bfull"] = bf_readers
            s.free["cfull"] = cf_readers
            if hh == 1:
                cci = self.ccq[g % 2]
                extra = [Ticket(cci, cci.n)] if g >= 2 else []
                t_cc = self.em.emit(
                    "gpsimd",
                    lambda e, g=g: e.collective_compute(
                        "AllGather", ALU.bypass,
                        replica_groups=[[0, 1], [2, 3], [4, 5], [6, 7]],
                        ins=[s.yag_in[g].opt()],
                        outs=[s.yag_all[g, :, :].opt()]),
                    waits=s.free[("ywr", g)] + extra, inc=cci)
                s.free[("yag", g)] = t_cc

    def dyn_dma(self, fn, waits=(), reg_src=0):
        i = self.qcount["sync"]
        self.qcount["sync"] = i + 1
        q = self.qsems["sync"][i % self.RQ]
        extra = [Ticket(q, q.n)] if i >= self.RQ else []
        s = self

        def op(e, i=i, reg_src=reg_src):
            with e.register(f"dyn{i}") as r:
                e.reg_load(r, s.meta_sb[0:1, reg_src:reg_src + 1])
                return fn(e, e.snap(r))
        return self.em.emit("sync", op, waits=list(waits) + extra,
                            inc=q, inck=16)

    def post_chunk(self, kk):
        s = self
        t_yag = [s.free[("yag", g)] for g in range(8)]
        if "ycp" not in s.free:
            def fyc(e, off):
                v = s.yag_all[:, :, :].rearrange("a p w -> (a p) w")
                return e.dma_start(out=s.yc_d[:, :],
                                   in_=v[bass.ds(off, 8192), :])
            s.free["ycp"] = self.dyn_dma(fyc, reg_src=3,
                                         waits=t_yag + [s.t_meta])
        t_ycp = s.free["ycp"]
        t_yp_wr = []
        for ps in range(2):
            c0 = ps * 512
            oc = kk * TC_W + c0
            for et in range(16):
                sl = et % 2
                ry = kk * 2048 + et * 128
                t_y = s.dma("sync", s.xsst[:, :],
                            s.yc_d[ry:ry + 128, c0:c0 + 512],
                            waits=[t_ycp, s.free.get("xsst")],
                            key="posty")
                rxs = self.agrow_xs(et)
                t_xs = s.dma("scalar", s.cg0[:, :],
                             s.ag_in[kk][rxs:rxs + 128,
                                         c0:c0 + 512],
                             waits=[s.free.get("cg0")], key="pxsl")
                t_z = s.dma("scalar", s.yst[:, :],
                            s.z_d[et * 128:(et + 1) * 128, oc:oc + 512],
                            waits=[s.free.get("ystf"),
                                   s.free.get(("zw", kk))[-1]], key="postz")
                t_y2 = s.v(lambda e, et=et: e.scalar_tensor_tensor(
                    s.xsst[:, :], s.cg0[:, :], s.dprm_t[:, et:et + 1],
                    s.xsst[:, :], ALU.mult, ALU.add),
                    waits=[t_y, t_xs, s.t_small])
                s.free["cg0"] = t_y2
                t_sz = s.s(lambda e: e.activation(
                    s.yst[:, :], s.yst[:, :], AF.Silu), waits=[t_z])
                t_yp = s.v(lambda e: e.tensor_mul(
                    s.ypb[:, :], s.xsst[:, :], s.yst[:, :]),
                    waits=[t_y2, t_sz, s.free.get("ypb")])
                s.free["xsst"] = t_yp
                s.free["ystf"] = t_yp
                t_w = s.dma("scalar",
                            s.ypost_d[et * 128:(et + 1) * 128, oc:oc + 512],
                            s.ypb[:, :], waits=[t_yp], key="ypw")
                s.free["ypb"] = t_w
                t_yp_wr.append(t_w)
        t_x2w = []
        for dmq in range(2):
            t_mv = [s.dma(
                "sync", s.xsb[:, :, :],
                s.outprojT[:, dmq * 512:(dmq + 1) * 512]
                .rearrange("(c p) w -> p c w", p=128),
                waits=[[s.free.get(("wop", c)) for c in range(16)],
                       [s.free.get(("xsbfree", c)) for c in range(16)],
                       s.t_wT_op], key="wop")]
            for tpp in range(4):
                t_gb = s.dma("sync", s.gst[:, 0:16, 0:256],
                             s.ypost_d[:, kk * TC_W + tpp * 256:
                                       kk * TC_W + (tpp + 1) * 256]
                             .rearrange("(c p) t -> p c t", p=128),
                             waits=t_yp_wr + [s.free.get("gst")],
                             key="gst")
                for sub in range(2):
                    tpo = tpp * 2 + sub
                    bnk = ["pT0", "pT1"][sub]
                    bank = self.banks[bnk]
                    tmm = None
                    for c in range(16):
                        tmm = s.t(lambda e, c=c, sub=sub, bank=bank:
                                  e.matmul(
                                      bank[:, 0:512],
                                      s.gst[:, c, sub * 128:(sub + 1) * 128],
                                      s.xsb[:, c, :],
                                      start=(c == 0), stop=(c == 15)),
                                  waits=([t_gb, t_mv[-1]] +
                                         ([self.bankw(bnk)] if c == 0
                                          else [])))
                    slx = sub
                    rr = kk * TC_W + tpo * 128
                    t_xl = s.dma("scalar", s.xres[:, slx, :],
                                 s.x1_d[PFX + rr:PFX + rr + 128,
                                        dmq * 512:(dmq + 1) * 512],
                                 waits=[s.free.get(("xres", slx))],
                                 key=("xres", slx))
                    t_ev = s.v(lambda e, bank=bank, slx=slx: e.tensor_add(
                        s.x1t[:, slx, :], bank[:, 0:512],
                        s.xres[:, slx, :]),
                        waits=[tmm, t_xl, s.free.get(("x1t", slx)),
                               s.free.get("ipst")])
                    self.bankset(bnk, t_ev)
                    s.free[("xres", slx)] = t_ev
                    t_w2 = s.dma("scalar",
                                 s.x2_d[rr:rr + 128,
                                        dmq * 512:(dmq + 1) * 512],
                                 s.x1t[:, slx, :], waits=[t_ev],
                                 key=("resw", slx))
                    s.free[("x1t", slx)] = t_w2
                    t_x2w.append(t_w2)
                s.free["gst"] = tmm
            for c in range(16):
                s.free[("wop", c)] = tmm
        s.free[("x2w", kk)] = t_x2w

    def finish(self):
        s = self
        if STAGE == "scan":
            waits = [s.free[("yag", g)] for g in range(8)]
            if "y" in s.dbg_out:
                for g in range(8):
                    t = s.dma("sync",
                              s.dbg_out["y"][:, g * TC_W:(g + 1) * TC_W],
                              s.yag_in[g][:, :], waits=waits)
                    waits = waits + [t]
            if "sc" in s.dbg_out:
                t = s.dma("sync", s.dbg_out["sc"][:, :], s.sc_d[0][:, :],
                          waits=waits)
                waits = waits + [t]
            t2 = s.dma("sync", s.out[:, :], s.x1_d[PFX:, :], waits=waits)
            self.em.emit("gpsimd", lambda e: e.nop(), waits=[t2])
            return
        if STAGE == "mamba_tok":
            waits = [s.free[("ag", k)] for k in range(N_TC)]
            if "ag" in s.dbg_out:
                for k in range(N_TC):
                    t = s.dma("sync",
                              s.dbg_out["ag"][:, k * TC_W:(k + 1) * TC_W],
                              s.ag_in[k][:, :], waits=waits)
                    waits = waits + [t]
            t2 = s.dma("sync", s.out[:, :], s.x1_d[PFX:, :], waits=waits)
            self.em.emit("gpsimd", lambda e: e.nop(), waits=[t2])
            return
        if STAGE == "ffn1":
            waits = [t for kk in range(N_TC) for t in s.x1_ready[kk]]
            if "x1" in s.dbg_out:
                t = s.dma("sync", s.dbg_out["x1"][:, :], s.x1_d[:, :],
                          waits=waits)
                waits = waits + [t]
            if "lnq" in s.dbg_out:
                t = s.dma("sync", s.dbg_out["lnq"][:, :], s.lnq_d[:, :],
                          waits=waits)
                waits = waits + [t]
            if "ag" in s.dbg_out:
                t = s.dma("sync", s.dbg_out["ag"][:, :],
                          s.graw_d[:, 0:TC_W + PFX], waits=waits)
                waits = waits + [t]
            if "ginq" in s.dbg_out:
                t = s.dma("sync", s.dbg_out["ginq"][:, :], s.ginq1_d[:, :],
                          waits=waits)
                waits = waits + [t]
            if "s2" in s.dbg_out:
                t = s.dma("sync", s.dbg_out["s2"][0:1, :], s.s2row1_d[:, :],
                          waits=waits)
                t = s.dma("sync", s.dbg_out["s2"][1:2, :], s.s1row1_d[:, :],
                          waits=waits)
                waits = waits + [t]
            t2 = s.dma("sync", s.out[:, :], s.x1_d[PFX:, :], waits=waits)
            self.em.emit("gpsimd", lambda e: e.nop(), waits=[t2])
            return
        waits = []
        for kk in range(2 * N_TC):
            waits += s.free[("outw", kk)]
        self.em.emit("gpsimd", lambda e: e.nop(), waits=waits)
        self.em.emit("sync", lambda e: e.nop(), waits=waits)


if __name__ == "__main__":
    print("library module")
